# version 04 (final)
# baseline (speedup 1.0000x reference)
import numpy as np
import ml_dtypes
import concourse.bass as bass
import concourse.mybir as mybir
from concourse.bass_utils import run_bass_kernel_spmd

F32 = mybir.dt.float32
BF16 = mybir.dt.bfloat16
I32 = mybir.dt.int32
ALU = mybir.AluOpType
ACT = mybir.ActivationFunctionType
AX = mybir.AxisListType.X

T, C, NH, DK = 2048, 1024, 16, 64
NT, NC = T // 128, C // 128
NEXP, SF, INNER = 16, 4096, 4096
SHIFT = 12.0
EPS = 1e-5
SBUF_BASE = 16896
SBUF_WIN = 225120 - 16896

NSUB = 8
CHANNELS = ("T", "V", "S", "G", "SY", "Gd", "SYd", "Sd")
PHYS = {"T": "tensor", "V": "vector", "S": "scalar", "G": "gpsimd",
        "SY": "sync", "Gd": "gpsimd", "SYd": "sync", "Sd": "scalar"}
UNIT = {"T": 1, "V": 1, "S": 1, "G": 1, "SY": 1, "Gd": 16, "SYd": 16, "Sd": 16}
DMACH = {"Gd", "SYd", "Sd"}


class Cmd:
    __slots__ = ("ch", "fn", "pub", "pubval", "deps", "idx", "tag", "sub", "occ")

    def __init__(self, ch, fn, idx, tag):
        self.ch, self.fn, self.idx, self.tag = ch, fn, idx, tag
        self.pub = ch in DMACH
        self.pubval = None
        self.deps = []
        self.sub = None
        self.occ = None


class Em:
    def __init__(self):
        self.cmds = []
        self.res = {}

    def emit(self, ch, fn, r=(), w=(), tag=""):
        c = Cmd(ch, fn, len(self.cmds), tag)
        phys = PHYS[ch]

        def dep(p):
            if p is None or p is c:
                return
            if PHYS[p.ch] == phys and p.ch not in DMACH:
                return
            p.pub = True
            c.deps.append(p)

        for key in w:
            e = self.res.get(key)
            if e is not None:
                dep(e[0])
                for rd in e[1]:
                    dep(rd)
            self.res[key] = [c, []]
        for key in r:
            e = self.res.get(key)
            if e is None:
                self.res[key] = [None, [c]]
            else:
                dep(e[0])
                e[1].append(c)
        self.cmds.append(c)
        return c

    def fence(self, fence_fn):
        last = {}
        for c in self.cmds:
            if c.tag.startswith("fence:b"):
                continue
            last[c.ch] = c
        self._fence_n = getattr(self, "_fence_n", 0) + 1
        key = f"__fb{self._fence_n}"
        central = self.emit("V", fence_fn, r=(), w=(key,), tag="fence:c")
        for ch, p in last.items():
            if p is central or (PHYS[p.ch] == "vector" and p.ch not in DMACH):
                continue
            p.pub = True
            central.deps.append(p)
        for ch in ("T", "S", "G", "SY"):
            self.emit(ch, lambda e: None, r=(key,), w=(), tag=f"fence:b{ch}")

    def finalize(self):
        counts = {ch: 0 for ch in CHANNELS}
        subcnt = {}
        for c in self.cmds:
            if c.ch in DMACH:
                counts[c.ch] += 1
                c.pubval = counts[c.ch]
                c.sub = (c.pubval - 1) % NSUB
                key = (c.ch, c.sub)
                subcnt[key] = subcnt.get(key, 0) + 1
                c.occ = subcnt[key]
            elif c.pub:
                counts[c.ch] += 1
                c.pubval = counts[c.ch]
        self.pub_totals = counts
        self.sub_totals = subcnt

    def replay(self, phys_name, eng, sems):
        waited = {}
        for c in self.cmds:
            if PHYS[c.ch] != phys_name:
                continue
            need = {}
            for p in c.deps:
                if p.ch in DMACH:
                    key = (p.ch, p.sub)
                    tgt = p.occ * 16
                else:
                    key = p.ch
                    tgt = p.pubval
                if need.get(key, 0) < tgt:
                    need[key] = tgt
            for key in sorted(need, key=str):
                if waited.get(key, 0) < need[key]:
                    eng.wait_ge(sems[key], need[key])
                    waited[key] = need[key]
            if c.ch in DMACH and c.occ > 1:
                key = (c.ch, c.sub)
                tgt = (c.occ - 1) * 16
                if waited.get(key, 0) < tgt:
                    eng.wait_ge(sems[key], tgt)
                    waited[key] = tgt
            inst = c.fn(eng)
            if c.ch in DMACH:
                assert inst is not None, f"dma cmd returned None: {c.tag}"
                inst.then_inc(sems[(c.ch, c.sub)], 16)
            elif c.pub:
                assert inst is not None, f"pub cmd returned None: {c.tag}"
                inst.then_inc(sems[c.ch], UNIT[c.ch])


def qk_perm():
    perm = np.zeros(64, dtype=np.int64)
    for blk in range(2):
        for i in range(16):
            pair = blk * 16 + i
            perm[blk * 32 + i] = 2 * pair
            perm[blk * 32 + 16 + i] = 2 * pair + 1
    return perm


def rope_tables():
    freqs = 10000.0 ** (-2.0 * np.arange(DK // 2, dtype=np.float64) / DK)
    ang = np.arange(T, dtype=np.float64)[None, :] * freqs[:, None]
    cos, sin = np.cos(ang), np.sin(ang)
    A = np.zeros((128, T), dtype=np.float64)
    B = np.zeros((128, T), dtype=np.float64)
    for blk in range(4):
        i0 = (blk % 2) * 16
        A[blk * 32:blk * 32 + 16] = cos[i0:i0 + 16]
        A[blk * 32 + 16:blk * 32 + 32] = cos[i0:i0 + 16]
        B[blk * 32:blk * 32 + 16] = -sin[i0:i0 + 16]
        B[blk * 32 + 16:blk * 32 + 32] = sin[i0:i0 + 16]
    return A.astype(ml_dtypes.bfloat16), B.astype(ml_dtypes.bfloat16)


def stage_consts(inputs):
    d = {}
    A, B = rope_tables()
    d["K_ropeA"], d["K_ropeB"] = np.ascontiguousarray(A), np.ascontiguousarray(B)
    p = np.arange(128)[:, None]
    f = np.arange(512)[None, :]
    masks = np.stack([(p <= f - 128 * m) for m in range(4)]
                     + [(p < f - 128 * m) for m in range(4)])
    d["K_masks"] = np.ascontiguousarray(masks.astype(ml_dtypes.bfloat16))
    d["K_ident"] = np.eye(128, dtype=ml_dtypes.bfloat16)
    d["K_identf"] = np.eye(128, dtype=np.float32)
    ones64 = np.zeros((128, 128), dtype=np.float32)
    ones64[64, :] = 1.0
    d["K_ones64"] = ones64
    d["K_ones128"] = np.ones((128, 128), dtype=np.float32)
    nsel = np.zeros((128, NEXP * 128), dtype=ml_dtypes.bfloat16)
    for n in range(NEXP):
        nsel[64 + n, n * 128:(n + 1) * 128] = 1.0
    d["K_nsel"] = nsel

    Wqkv = np.asarray(inputs["L0_attn_W_qkv"], dtype=np.float32)
    bqkv = np.asarray(inputs["L0_attn_b_qkv"], dtype=np.float32)
    perm = qk_perm()
    Wq = Wqkv[:, 0:1024].reshape(C, 16, 64)[:, :, perm].reshape(C, 1024)
    Wk = Wqkv[:, 1024:2048].reshape(C, 16, 64)[:, :, perm].reshape(C, 1024)
    d["K_Wqk"] = np.ascontiguousarray(np.concatenate([Wq, Wk], axis=1))
    d["K_Wv"] = np.ascontiguousarray(Wqkv[:, 2048:3072])
    bq = bqkv[0:1024].reshape(16, 64)[:, perm].reshape(1024)
    bk = bqkv[1024:2048].reshape(16, 64)[:, perm].reshape(1024)
    d["K_bqk"] = np.ascontiguousarray(np.concatenate([bq, bk]).reshape(16, 128).T.astype(np.float32))
    d["K_bv_rep"] = np.ascontiguousarray(np.broadcast_to(bqkv[2048:3072], (128, C)).astype(np.float32))
    d["K_Wproj0"] = np.ascontiguousarray(np.asarray(inputs["L0_attn_W_proj"], np.float32))
    d["K_bproj0_rep"] = np.ascontiguousarray(
        np.broadcast_to(np.asarray(inputs["L0_attn_b_proj"], np.float32), (128, C)))

    d["K_WrT"] = np.ascontiguousarray(np.asarray(inputs["L0_ffn_router_w"], np.float32).T)
    d["K_gup"] = np.ascontiguousarray(np.asarray(inputs["L0_ffn_gate_up_proj"], np.float32))
    d["K_down"] = np.ascontiguousarray(np.asarray(inputs["L0_ffn_down_proj"], np.float32))
    d["K_shgT"] = np.ascontiguousarray(np.asarray(inputs["L0_ffn_shared_gate_w"], np.float32).T)
    d["K_shuT"] = np.ascontiguousarray(np.asarray(inputs["L0_ffn_shared_up_w"], np.float32).T)
    d["K_shdT"] = np.ascontiguousarray(np.asarray(inputs["L0_ffn_shared_down_w"], np.float32).T)

    d["K_W1qk"] = np.ascontiguousarray(np.asarray(inputs["L1_attn_W_attn"], np.float32))
    d["K_W1k2"] = np.ascontiguousarray(np.asarray(inputs["L1_attn_W_k2"], np.float32))
    d["K_W1proj"] = np.ascontiguousarray(np.asarray(inputs["L1_attn_W_proj"], np.float32))

    d["K_wgT"] = np.ascontiguousarray(np.asarray(inputs["L1_ffn_w_gate"], np.float32).T)
    d["K_wuT"] = np.ascontiguousarray(np.asarray(inputs["L1_ffn_w_up"], np.float32).T)
    d["K_wdT"] = np.ascontiguousarray(np.asarray(inputs["L1_ffn_w_down"], np.float32).T)
    return d


class MK:
    def __init__(self, nc, stages=4):
        self.nc = nc
        self.stages = stages
        self.em = Em()
        self._off_hwm = 0
        self.declare_io()
        self.alloc_sbuf()
        self.alloc_psum()

    def declare_io(self):
        nc = self.nc
        self.x_in = nc.dram_tensor("x", [T, C], F32, kind="ExternalInput").ap()
        self.out = nc.dram_tensor("out", [T, C], F32, kind="ExternalOutput").ap()
        shapes = {
            "K_ropeA": ([128, T], BF16), "K_ropeB": ([128, T], BF16),
            "K_masks": ([8, 128, 512], BF16),
            "K_ident": ([128, 128], BF16), "K_identf": ([128, 128], F32),
            "K_ones64": ([128, 128], F32), "K_ones128": ([128, 128], F32),
            "K_nsel": ([128, NEXP * 128], BF16),
            "K_Wqk": ([C, 2048], F32), "K_Wv": ([C, C], F32),
            "K_bqk": ([128, 16], F32), "K_bv_rep": ([128, C], F32),
            "K_Wproj0": ([C, C], F32), "K_bproj0_rep": ([128, C], F32),
            "K_WrT": ([C, 16], F32),
            "K_gup": ([NEXP, C, 2048], F32), "K_down": ([NEXP, 1024, C], F32),
            "K_shgT": ([C, SF], F32), "K_shuT": ([C, SF], F32), "K_shdT": ([SF, C], F32),
            "K_W1qk": ([C, 2048], F32), "K_W1k2": ([C, C], F32), "K_W1proj": ([C, C], F32),
            "K_wgT": ([C, INNER], F32), "K_wuT": ([C, INNER], F32), "K_wdT": ([INNER, C], F32),
        }
        self.k = {n: nc.dram_tensor(n, s, dt, kind="ExternalInput").ap()
                  for n, (s, dt) in shapes.items()}
        import os
        self.debug = os.environ.get("MK_DEBUG", "0") == "1"
        if self.debug:
            self.dbg_ot = nc.dram_tensor("dbg_ot", [128, NC * T], F32, kind="ExternalOutput").ap()
            self.dbg_qt = nc.dram_tensor("dbg_qt", [128, 2 * T], F32, kind="ExternalOutput").ap()
            self.dbg_kt = nc.dram_tensor("dbg_kt", [128, 2 * T], F32, kind="ExternalOutput").ap()
            self.dbg_sum = nc.dram_tensor("dbg_sum", [128, 512], F32, kind="ExternalOutput").ap()
            self.dbg_xt = nc.dram_tensor("dbg_xt", [128, NC * T], F32, kind="ExternalOutput").ap()
            self.dbg_st = nc.dram_tensor("dbg_st", [128, 64], F32, kind="ExternalOutput").ap()
        self.xres_dram = nc.dram_tensor("xres_scr", [T, C], F32, kind="Internal").ap()
        self.nw_dram = [
            nc.dram_tensor("nwg_scr", [C, INNER], BF16, kind="Internal").ap(),
            nc.dram_tensor("nwu_scr", [C, INNER], BF16, kind="Internal").ap(),
            nc.dram_tensor("nwd_scr", [INNER, C], BF16, kind="Internal").ap(),
        ]

    def sb(self, name, off, shape, dt):
        nbytes = int(np.prod(shape[1:])) * mybir.dt.size(dt)
        assert off + nbytes <= SBUF_WIN, f"{name}: {off}+{nbytes} > {SBUF_WIN}"
        self._off_hwm = max(self._off_hwm, off + nbytes)
        return self.nc.alloc_sbuf_tensor_at(name, list(shape), dt, offset=SBUF_BASE + off)

    def alloc_sbuf(self):
        sb = self.sb
        self.XT = sb("XT", 0, [128, NC * T], BF16)
        S1 = 32768
        self.V1 = sb("V1", S1, [128, NT * NH * 65], BF16)
        self.X2Tn = sb("X2Tn", S1, [128, 8 * 1024], BF16)
        self.HSHE = sb("HSHE", S1 + 16384, [128, 16 * 512], BF16)
        self.NHT = sb("NHT", S1, [128, 2 * 32 * 128], BF16)
        S2 = 66048
        self.OT = sb("OT", S2, [128, NC * T], BF16)
        self.MWgu = sb("MWgu", S2, [128, 4 * 2048], BF16)
        self.MWd = sb("MWd", S2 + 16384, [128, 8 * 1024], BF16)
        self.NWgu = sb("NWgu", S2, [128, 4 * 512], BF16)
        self.NWd = sb("NWd", S2 + 4096, [128, 4 * 512], BF16)
        S3 = 98816
        self.ropeQ = sb("ropeQ", S3, [128, T], F32)
        self.ropeA = sb("ropeA", S3 + 8192, [128, T], BF16)
        self.ropeB = sb("ropeB", S3 + 12288, [128, T], BF16)
        self.Wqk_sb = sb("Wqk_sb", S3 + 16384, [128, 2 * 1024], BF16)
        self.Wbig = sb("Wbig", S3 + 20480, [128, NC * 1024], BF16)
        self.sums = sb("sums", S3 + 36864, [128, 512], F32)
        self.ropetmp = sb("ropetmp", S3 + 38912, [128, 512], BF16)
        self.brep = sb("brep", S3 + 39936, [128, C], F32)
        self.brep2 = sb("brep2", S3 + 44032, [128, C], F32)
        self.wq_m = sb("wq_m", S3 + 48128, [128, 512], F32)
        self.wq_u = sb("wq_u", S3 + 50176, [128, 512], F32)
        self.wq_o = sb("wq_o", S3 + 52224, [128, 512], BF16)
        self.recip_sb = sb("recip_sb", S3 + 53248, [128, 512], F32)
        self.rec_tmp = sb("rec_tmp", S3 + 55296, [128, 512], F32)
        self.Ebuf = sb("Ebuf", S3, [128, NT * NH * 64], BF16)
        self.QA = sb("QA", S3 + 32768, [128, T], BF16)
        self.KA = sb("KA", S3 + 36864, [128, T], BF16)
        self.W1buf = sb("W1buf", S3 + 40960, [128, 2 * 1024], BF16)
        self.W2buf = sb("W2buf", S3 + 45056, [128, 2 * 1024], BF16)
        self.ytmp = sb("ytmp", S3 + 49152, [128, 1024], BF16)
        self.vsh = sb("vsh", S3 + 51200, [128, 1024], BF16)
        self.ACCB = sb("ACCB", S3, [128, NT * 1024], BF16)
        self.MOEh = sb("MOEh", S3 + 32768, [128, 16 * 512], BF16)
        self.sTb = sb("sTb", S3 + 49152, [128, T], BF16)
        self.spad = sb("spad", S3 + 53248, [128, 80], F32)
        self.xchunk = sb("xchunk", S3 + 53568, [128, 2 * 128], F32)
        self.sig_sb = sb("sig_sb", S3 + 54592, [128, 16], F32)
        self.oneh_sb = sb("oneh_sb", S3 + 54656, [128, 16], F32)
        self.MOEsg = sb("MOEsg", S3 + 54784, [128, 512], F32)
        self.H2 = sb("H2", S3, [128, 2 * INNER], BF16)
        self.Q1 = sb("Q1", S3 + 16384, [128, 1024], I32)
        self.Q2 = sb("Q2", S3 + 20480, [128, 1024], F32)
        self.Q3 = sb("Q3", S3 + 24576, [128, 1024], F32)
        self.Q4 = sb("Q4", S3 + 28672, [128, 1024], F32)
        self.Q5 = sb("Q5", S3 + 32768, [128, 1024], F32)
        self.NHb = sb("NHb", S3 + 36864, [128, INNER], BF16)
        self.SILG = sb("SILG", S3 + 45056, [128, 2 * 512], BF16)
        self.XQ = sb("XQ", S3 + 49152, [128, 1024], BF16)
        S4 = 156160
        self.QT = sb("QT", S4, [128, 2 * T], BF16)
        self.KT = sb("KT", S4 + 8192, [128, 2 * T], BF16)
        self.PT = sb("PT", S4 + 16384, [128, 4 * 512], BF16)
        self.masks = sb("masks", S4 + 20480, [128, 8 * 512], BF16)
        self.nsel = sb("nsel", S4, [128, NEXP * 128], BF16)
        FX = 184832
        self.ident = sb("ident", FX, [128, 128], BF16)
        self.identf = sb("identf", FX + 256, [128, 128], F32)
        self.ones64 = sb("ones64", FX + 768, [128, 128], F32)
        self.ones128 = sb("ones128", FX + 1280, [128, 128], F32)
        self.bqk = sb("bqk", FX + 1792, [128, 16], F32)
        self.WrT_sb = sb("WrT_sb", FX + 1856, [128, NC * 16], F32)
        self.xres_t = sb("xres_t", FX + 2368, [128, 2 * C], F32)
        self.xn_f = sb("xn_f", FX + 10560, [128, 2 * C], F32)
        self.xn_b = sb("xn_b", FX + 18752, [128, 2 * C], BF16)
        self.stats = sb("stats", FX + 22848, [128, 64], F32)
        self.theta = sb("theta", FX + 23104, [128, 16], F32)
        self.csc = sb("csc", FX + 23168, [128, 32], F32)
        self.fconst = sb("fconst", FX + 23296, [128, 4], F32)
        assert FX + 23312 <= SBUF_WIN

    def alloc_psum(self):
        nc = self.nc
        self.ps = [nc.alloc_psum_tensor(f"ps{i}", [128, 512], F32) for i in range(6)]
        self.psT1 = nc.alloc_psum_tensor("psT1", [128, 1024], BF16)
        self.psT2 = nc.alloc_psum_tensor("psT2", [128, 1024], BF16)

    def T_(self, fn, r=(), w=(), tag=""):
        return self.em.emit("T", fn, r, w, tag)

    def V_(self, fn, r=(), w=(), tag=""):
        return self.em.emit("V", fn, r, w, tag)

    def S_(self, fn, r=(), w=(), tag=""):
        return self.em.emit("S", fn, r, w, tag)

    def G_(self, fn, r=(), w=(), tag=""):
        return self.em.emit("G", fn, r, w, tag)

    def dma(self, q, out_ap, in_ap, r=(), w=(), tag=""):
        return self.em.emit(q, lambda e: e.dma_start(out_ap, in_ap), r, w, tag)

    def load_consts(self):
        k = self.k
        for t, key in [(self.ropeA, "K_ropeA"), (self.ropeB, "K_ropeB"),
                       (self.ident, "K_ident"), (self.identf, "K_identf"),
                       (self.ones64, "K_ones64"), (self.ones128, "K_ones128"),
                       (self.bqk, "K_bqk")]:
            self.dma("SYd", t[:], k[key], w=[t.name], tag=f"ld:{t.name}")
        self.V_(lambda e: e.memset(self.fconst[:, 0:2], -SHIFT), w=["fconst"], tag="fc")
        self.V_(lambda e: e.memset(self.stats[:], 0.0),
                w=["st0_ss", "st0_sq", "st0_sc", "st0_sb", "st0_t1",
                   "st1_ss", "st1_sq", "st1_sc", "st1_sb", "st1_t1",
                   "rmax", "wqacc", "wqr0", "wqr1", "wqmn"], tag="stz")
        self.V_(lambda e: e.memset(self.csc[:], 0.0),
                w=["am", "cx", "sx", "ah", "ch", "sh", "cg0", "cg1", "cu0", "cu1", "cf0", "cf1"],
                tag="cscz")
        self.V_(lambda e: e.memset(self.theta[:], 0.0), w=["theta"], tag="thz")
        for mi in range(8):
            self.dma("SYd", self.masks[:, mi * 512:(mi + 1) * 512], k["K_masks"][mi],
                     w=["masks"], tag=f"ld:masks{mi}")
        self.dma("SYd", self.WrT_sb[:], k["K_WrT"].rearrange("(c p) n -> p c n", p=128),
                 w=["WrT_sb"], tag="ld:wrt")

    def ln_tile(self, xt_ap, slot, want_f32, tag):
        st = self.stats
        base = slot * 16
        ss = st[:, base:base + 2]
        sq = st[:, base + 4:base + 6]
        t1 = st[:, base + 8:base + 10]
        sc = st[:, base + 12:base + 14]
        sbb = st[:, base + 14:base + 16]
        xr = f"xres_t{slot}"
        K = lambda n: f"st{slot}_{n}"
        self.V_(lambda e: e.reduce_sum(ss[:, 0:1], xt_ap, AX), r=[xr], w=[K("ss")], tag=tag + ":sum")
        sq_out = self.xn_f[:, slot * C:(slot + 1) * C]
        self.S_(lambda e: e.activation(sq_out, xt_ap, ACT.Square),
                r=[xr], w=[f"xn_f{slot}"], tag=tag + ":sq")
        self.V_(lambda e: e.reduce_sum(sq[:, 0:1], sq_out, AX),
                r=[f"xn_f{slot}"], w=[K("sq")], tag=tag + ":sqr")
        pad = self.stats[:, 62:64]
        self.V_(lambda e: e.memset(pad, 0.0), tag=tag + ":pa")
        self.V_(lambda e: e.memset(pad, 0.0), tag=tag + ":pb")
        self.V_(lambda e: e.tensor_scalar_mul(ss, ss, 1.0 / C), r=[K("ss")], w=[K("ss")], tag=tag)
        self.V_(lambda e: e.tensor_scalar_mul(sq, sq, 1.0 / C), r=[K("sq")], w=[K("sq")], tag=tag)
        self.V_(lambda e: e.scalar_tensor_tensor(t1, ss, 1.0, ss, ALU.mult, ALU.mult),
                r=[K("ss")], w=[K("t1")], tag=tag)
        self.V_(lambda e: e.tensor_sub(sq, sq, t1), r=[K("sq"), K("t1")], w=[K("sq")], tag=tag)
        self.V_(lambda e: e.tensor_scalar_add(sq, sq, EPS), r=[K("sq")], w=[K("sq")], tag=tag)
        self.S_(lambda e: e.activation(t1, sq, ACT.Sqrt), r=[K("sq")], w=[K("t1")], tag=tag)
        self.V_(lambda e: e.reciprocal(sc, t1), r=[K("t1")], w=[K("sc")], tag=tag)
        self.V_(lambda e: e.scalar_tensor_tensor(sbb, ss, -1.0, sc, ALU.mult, ALU.mult),
                r=[K("ss"), K("sc")], w=[K("sb")], tag=tag)
        xnb = self.xn_b[:, slot * C:(slot + 1) * C]
        self.S_(lambda e: e.activation(xnb, xt_ap, ACT.Identity, bias=sbb[:, 0:1], scale=sc[:, 0:1]),
                r=[xr, K("sc"), K("sb")], w=[f"xn_b{slot}"], tag=tag + ":ap")
        if want_f32:
            xnf = self.xn_f[:, slot * C:(slot + 1) * C]
            self.S_(lambda e: e.activation(xnf, xt_ap, ACT.Identity, bias=sbb[:, 0:1], scale=sc[:, 0:1]),
                    r=[xr, K("sc"), K("sb")], w=[f"xn_f{slot}"], tag=tag + ":apf")

    def ln_and_transpose(self, x_src, fp32_tr=False, chunk_cb=None, tile_cb=None, tag="ln"):
        for j in list(range(NT)) + [0, 1]:
            slot = j % 2
            xt_ap = self.xres_t[:, slot * C:(slot + 1) * C]
            self.dma("SYd", xt_ap, x_src[j * 128:(j + 1) * 128, :],
                     w=[f"xres_t{slot}"], tag=f"{tag}:ldx{j}")
            self.ln_tile(xt_ap, slot, fp32_tr, f"{tag}{j}")
            src = (self.xn_f if fp32_tr else self.xn_b)[:, slot * C:(slot + 1) * C]
            srckey = f"xn_{'f' if fp32_tr else 'b'}{slot}"
            for c in range(NC):
                ii = (j * NC + c) % 2
                if fp32_tr:
                    ps, pr = self.ps[4 + ii], f"ps{4 + ii}"
                    ident, ikey = self.identf, "identf"
                else:
                    ps, pr = (self.psT1, self.psT2)[ii], f"psT{ii + 1}"
                    ident, ikey = self.ident, "ident"
                self.T_(lambda e, c=c, src=src, ps=ps, ident=ident:
                        e.transpose(ps[:, 0:128], src[:, c * 128:(c + 1) * 128], ident[:]),
                        r=[srckey, ikey], w=[pr], tag=f"{tag}:tr{j}.{c}")
                dst = self.XT[:, c * T + j * 128:c * T + (j + 1) * 128]
                self.S_(lambda e, dst=dst, ps=ps: e.activation(dst, ps[:, 0:128], ACT.Copy),
                        r=[pr], w=["XT"], tag=f"{tag}:te{j}.{c}")
                if chunk_cb is not None:
                    chunk_cb(j, slot, c, ps, pr)
            if tile_cb is not None:
                tile_cb(j, slot)
            if getattr(self, "debug", False) and j == 1 and tag == "ln0":
                self.dma("SYd", self.dbg_st, self.stats[:],
                         r=["st0_ss", "st0_sq", "st0_sc", "st0_sb", "st1_ss", "st1_sq", "st1_sc", "st1_sb"],
                         w=["dbgst"], tag="dbg:st01")

    def v1_ones(self):
        v1r = self.V1[:].rearrange("p (jh w) -> p jh w", w=65)
        self.V_(lambda e: e.memset(v1r[:, :, 64:65], 1.0), w=["V1"], tag="v1ones")

    def v1_assemble_l0(self):
        k = self.k
        self.dma("SYd", self.brep[:], k["K_bv_rep"], w=["brep"], tag="ld:bv")
        self.v1_ones()
        self.dma("Gd", self.Wbig[:], k["K_Wv"].rearrange("(c p) n -> p c n", p=128),
                 w=["Wbig"], tag="ld:Wv")
        v1r = self.V1[:].rearrange("p (j h w) -> p j h w", j=NT, h=NH, w=65)
        for j in range(NT):
            for n in range(2):
                ps, pr = self.ps[4 + n], f"ps{4 + n}"
                for c in range(NC):
                    lhs = self.XT[:, c * T + j * 128:c * T + (j + 1) * 128]
                    rhs = self.Wbig[:, c * 1024 + n * 512:c * 1024 + (n + 1) * 512]
                    self.T_(lambda e, ps=ps, lhs=lhs, rhs=rhs, c=c:
                            e.matmul(ps[:], lhs, rhs, start=(c == 0), stop=(c == NC - 1)),
                            r=["XT", "Wbig"], w=[pr], tag=f"v{j}.{n}.{c}")
                dst = v1r[:, j, n * 8:(n + 1) * 8, 0:64]
                bias = self.brep[:].rearrange("p (h w) -> p h w", h=16, w=64)[:, n * 8:(n + 1) * 8, :]
                self.V_(lambda e, dst=dst, ps=ps, bias=bias:
                        e.tensor_tensor(dst, ps[:].rearrange("p (h w) -> p h w", h=8, w=64),
                                        bias, ALU.add),
                        r=[pr, "brep"], w=["V1"], tag=f"vev{j}.{n}")

    def qk_group(self, g, wsrc, rope, tag, wbuf=None, wpfx="Wqk"):
        slot = g % 2
        wt = wbuf if wbuf is not None else self.Wqk_sb
        for qk in range(2):
            wslot = (2 * g + qk) % 2
            wsb = wt[:, wslot * 1024:(wslot + 1) * 1024]
            col0 = qk * 1024 + g * 128
            self.dma("Gd", wsb, wsrc[:, col0:col0 + 128].rearrange("(c p) n -> p c n", p=128),
                     w=[f"{wpfx}{wslot}"], tag=f"{tag}:lw{g}.{qk}")
            dstT = (self.QT, self.KT)[qk]
            dkey = ("QT", "KT")[qk] + str(slot)
            for qc in range(4):
                ps, pr = self.ps[4 + qc % 2], f"ps{4 + qc % 2}"
                for c in range(NC):
                    self.T_(lambda e, ps=ps, wsb=wsb, c=c, qc=qc:
                            e.matmul(ps[:], wsb[:, c * 128:(c + 1) * 128],
                                     self.XT[:, c * T + qc * 512:c * T + (qc + 1) * 512],
                                     start=(c == 0), stop=(c == NC - 1)),
                            r=[f"{wpfx}{wslot}", "XT"], w=[pr], tag=f"{tag}:m{g}.{qk}.{qc}.{c}")
                if rope:
                    bias = self.bqk[:, 8 * qk + g:8 * qk + g + 1]
                    dst = self.ropeQ[:, qc * 512:(qc + 1) * 512]
                    self.S_(lambda e, dst=dst, ps=ps, bias=bias:
                            e.activation(dst, ps[:], ACT.Identity, bias=bias),
                            r=[pr, "bqk"], w=[f"ropeQ{qc}"], tag=f"{tag}:e{g}.{qk}.{qc}")
                else:
                    dst = dstT[:, slot * T + qc * 512:slot * T + (qc + 1) * 512]
                    self.S_(lambda e, dst=dst, ps=ps: e.activation(dst, ps[:], ACT.Copy),
                            r=[pr], w=[dkey], tag=f"{tag}:e{g}.{qk}.{qc}")
            if rope:
                shuf = [(i + 16) % 32 for i in range(32)]
                out = dstT[:, slot * T:(slot + 1) * T]
                for qc in range(4):
                    seg = slice(qc * 512, (qc + 1) * 512)
                    self.V_(lambda e, seg=seg:
                            e.stream_shuffle(self.wq_m[:], self.ropeQ[:, seg], shuf),
                            r=[f"ropeQ{qc}"], w=["wqm"], tag=f"{tag}:sh{g}.{qk}.{qc}")
                    self.V_(lambda e, seg=seg:
                            e.tensor_mul(self.wq_u[:], self.wq_m[:], self.ropeB[:, seg]),
                            r=["wqm", "ropeB"], w=["wqu"], tag=f"{tag}:rb{g}.{qk}.{qc}")
                    self.V_(lambda e, seg=seg:
                            e.tensor_mul(self.wq_m[:], self.ropeQ[:, seg], self.ropeA[:, seg]),
                            r=[f"ropeQ{qc}", "ropeA"], w=["wqm"], tag=f"{tag}:ra{g}.{qk}.{qc}")
                    self.V_(lambda e, seg=seg, out=out:
                            e.tensor_add(out[:, seg], self.wq_m[:], self.wq_u[:]),
                            r=["wqm", "wqu"], w=[dkey], tag=f"{tag}:ro{g}.{qk}.{qc}")

    def attn_group(self, g, strictmask, tag):
        slot = g % 2
        v1r = self.V1[:].rearrange("p (j hw) -> p j hw", j=NT, hw=NH * 65)
        for hh in range(2):
            h = 2 * g + hh
            hb = hh * 64
            qt = self.QT[:, slot * T:(slot + 1) * T]
            kt = self.KT[:, slot * T:(slot + 1) * T]
            qs, ks = f"QT{slot}", f"KT{slot}"
            cch = h // 2
            for qc in range(4):
                pso, po = self.ps[2 + (h * 4 + qc) % 2], f"ps{2 + (h * 4 + qc) % 2}"
                nkt = 4 * (qc + 1)
                for kti in range(nkt):
                    pss, pr = self.ps[kti % 2], f"ps{kti % 2}"
                    self.T_(lambda e, pss=pss, kt=kt, qt=qt, hb=hb, kti=kti, qc=qc:
                            e.matmul(pss[:], kt[hb:hb + 64, kti * 128:(kti + 1) * 128],
                                     qt[hb:hb + 64, qc * 512:(qc + 1) * 512],
                                     start=True, stop=True),
                            r=[qs, ks], w=[pr], tag=f"{tag}:s{h}.{qc}.{kti}")
                    pts = kti % 4
                    pt = self.PT[:, pts * 512:(pts + 1) * 512]
                    self.S_(lambda e, pt=pt, pss=pss:
                            e.activation(pt, pss[:], ACT.Exp, bias=self.fconst[:, 0:1], scale=0.125),
                            r=[pr, "fconst"], w=[f"PT{pts}"], tag=f"{tag}:x{h}.{qc}.{kti}")
                    m = kti - 4 * qc
                    if m >= 0:
                        mo = ((4 + m) if strictmask else m) * 512
                        self.V_(lambda e, pt=pt, mo=mo:
                                e.tensor_mul(pt, pt, self.masks[:, mo:mo + 512]),
                                r=[f"PT{pts}", "masks"], w=[f"PT{pts}"], tag=f"{tag}:k{h}.{qc}.{kti}")
                    v1 = v1r[:, kti, h * 65:h * 65 + 65]
                    self.T_(lambda e, pso=pso, v1=v1, pt=pt, kti=kti, nkt=nkt:
                            e.matmul(pso[0:65, :], v1, pt, start=(kti == 0), stop=(kti == nkt - 1)),
                            r=[f"PT{pts}", "V1"], w=[po], tag=f"{tag}:p{h}.{qc}.{kti}")
                idx = h * 4 + qc
                sslot = (self.sums[64:65, :], self.recip_sb[64:65, :])[idx % 2]
                skey = f"sumslot{idx % 2}"
                self.S_(lambda e, pso=pso, sslot=sslot:
                        e.activation(sslot, pso[64:65, :], ACT.Copy),
                        r=[po], w=[skey], tag=f"{tag}:m{h}.{qc}")
                self.dma("Sd", self.rec_tmp[idx:idx + 1, :], sslot,
                         r=[skey], w=[f"sumsall{idx}"], tag=f"{tag}:g{h}.{qc}")
                dst = self.OT[hb:hb + 64, cch * T + qc * 512:cch * T + (qc + 1) * 512]
                self.S_(lambda e, dst=dst, pso=pso:
                        e.activation(dst, pso[0:64, :], ACT.Copy),
                        r=[po], w=["OT"], tag=f"{tag}:o{h}.{qc}")

    def attn_norm_finish(self, tag):
        self.V_(lambda e: e.reciprocal(self.recip_sb[0:64, :], self.rec_tmp[0:64, :]),
                r=[f"sumsall{i}" for i in range(64)], w=["recipall"], tag=f"{tag}:rcp")
        for h in range(NH):
            hb = (h % 2) * 64
            cch = h // 2
            for qc in range(4):
                idx = h * 4 + qc
                sslot = (self.sums[64:65, :], self.recip_sb[64:65, :])[idx % 2]
                skey = f"sumslot{idx % 2}"
                self.dma("Sd", sslot, self.recip_sb[idx:idx + 1, :],
                         r=["recipall"], w=[skey], tag=f"{tag}:b{h}.{qc}")
                psr, prr = self.ps[4 + idx % 2], f"ps{4 + idx % 2}"
                self.T_(lambda e, psr=psr, sslot=sslot:
                        e.matmul(psr[:], self.ones64[64:65, 0:128], sslot,
                                 start=True, stop=True),
                        r=[skey, "ones64"], w=[prr], tag=f"{tag}:r{h}.{qc}")
                dst = self.OT[hb:hb + 64, cch * T + qc * 512:cch * T + (qc + 1) * 512]
                self.V_(lambda e, dst=dst, psr=psr:
                        e.tensor_tensor(dst, dst, psr[0:64, :], ALU.mult),
                        r=[prr, "OT"], w=["OT"], tag=f"{tag}:n{h}.{qc}")

    def proj_residual(self, wname, bias_key, x_src, x_dst, tag):
        k = self.k
        self.dma("Gd", self.Wbig[:], k[wname].rearrange("(c p) n -> p c n", p=128),
                 w=["Wbig"], tag=f"{tag}:lw")
        if bias_key:
            self.dma("SYd", self.brep2[:], k[bias_key], w=["brep2"], tag=f"{tag}:lb")
        for j in range(NT):
            slot = j % 2
            xt_ap = self.xres_t[:, slot * C:(slot + 1) * C]
            self.dma("SYd", xt_ap, x_src[j * 128:(j + 1) * 128, :],
                     w=[f"xres_t{slot}"], tag=f"{tag}:lx{j}")
            xo = self.xn_f[:, slot * C:(slot + 1) * C]
            for n in range(2):
                ps, pr = self.ps[4 + n], f"ps{4 + n}"
                for c in range(NC):
                    lhs = self.OT[:, c * T + j * 128:c * T + (j + 1) * 128]
                    rhs = self.Wbig[:, c * 1024 + n * 512:c * 1024 + (n + 1) * 512]
                    self.T_(lambda e, ps=ps, lhs=lhs, rhs=rhs, c=c:
                            e.matmul(ps[:], lhs, rhs, start=(c == 0), stop=(c == NC - 1)),
                            r=["OT", "Wbig"], w=[pr], tag=f"{tag}:m{j}.{n}.{c}")
                seg = slice(n * 512, (n + 1) * 512)
                if bias_key:
                    self.V_(lambda e, ps=ps, seg=seg, xo=xo:
                            e.tensor_add(xo[:, seg], ps[:], self.brep2[:, seg]),
                            r=[pr, "brep2"], w=[f"xn_f{slot}"], tag=f"{tag}:b{j}.{n}")
                    self.V_(lambda e, seg=seg, xo=xo, xt_ap=xt_ap:
                            e.tensor_add(xo[:, seg], xo[:, seg], xt_ap[:, seg]),
                            r=[f"xn_f{slot}", f"xres_t{slot}"], w=[f"xn_f{slot}"],
                            tag=f"{tag}:a{j}.{n}")
                else:
                    self.V_(lambda e, ps=ps, seg=seg, xo=xo, xt_ap=xt_ap:
                            e.tensor_add(xo[:, seg], ps[:], xt_ap[:, seg]),
                            r=[pr, f"xres_t{slot}"], w=[f"xn_f{slot}"], tag=f"{tag}:a{j}.{n}")
            self.dma("SYd", x_dst[j * 128:(j + 1) * 128, :], xo,
                     r=[f"xn_f{slot}"], w=[f"xdst{j}"], tag=f"{tag}:sx{j}")

    def l0_attn(self, x_dst):
        self.ln_and_transpose(self.x_in, tag="ln0")
        self.v1_assemble_l0()
        for g in range(8):
            self.qk_group(g, self.k["K_Wqk"], rope=True, tag="a0")
            if g > 0:
                self.attn_group(g - 1, strictmask=False, tag="a0")
        self.attn_group(7, strictmask=False, tag="a0")
        self.attn_norm_finish("a0")
        if self.debug:
            self.dma("Gd", self.dbg_xt, self.XT[:], r=["XT"], w=["dbg"], tag="dbg:xt")

            self.dma("Gd", self.dbg_ot, self.OT[:], r=["OT"], w=["dbg"], tag="dbg:ot")
            self.dma("Gd", self.dbg_qt, self.QT[:], r=["QT0", "QT1"], w=["dbg"], tag="dbg:qt")
            self.dma("Gd", self.dbg_kt, self.KT[:], r=["KT0", "KT1"], w=["dbg"], tag="dbg:kt")
            self.dma("SYd", self.dbg_sum[0:64], self.rec_tmp[0:64, :],
                     r=[f"sumsall{i}" for i in range(64)], w=["dbg"], tag="dbg:sum")
        self.proj_residual("K_Wproj0", "K_bproj0_rep", self.x_in, x_dst, tag="p0")

    def moe(self, x_src, x_dst):
        k = self.k
        self.dma("SYd", self.nsel[:], k["K_nsel"], w=["nsel"], tag="ld:nsel")
        psr, prr = self.ps[2], "ps2"

        def chunk_cb(j, slot, c, ps, pr):
            xs = (j * NC + c) % 2
            xc = self.xchunk[:, xs * 128:(xs + 1) * 128]
            self.S_(lambda e, xc=xc, ps=ps: e.activation(xc, ps[:, 0:128], ACT.Copy),
                    r=[pr], w=[f"xch{xs}"], tag=f"rt:x{j}.{c}")
            self.T_(lambda e, xc=xc, c=c, j=j:
                    e.matmul(psr[:, j * 16:(j + 1) * 16], xc,
                             self.WrT_sb[:, c * 16:(c + 1) * 16],
                             start=(c == 0), stop=(c == NC - 1)),
                    r=[f"xch{xs}", "WrT_sb"], w=[prr], tag=f"rt:m{j}.{c}")

        self.ln_and_transpose(x_src, fp32_tr=True, chunk_cb=chunk_cb, tag="ln1")

        self.V_(lambda e: e.memset(self.spad[:, 0:80], 0.0), w=["spad"], tag="rt:z")
        for j in range(NT):
            lg = psr[:, j * 16:(j + 1) * 16]
            nmx = self.stats[:, 46:48]
            self.V_(lambda e, lg=lg, nmx=nmx: e.reduce_max(nmx[:, 0:1], lg, AX, negate=True),
                    r=[prr], w=["rmax"], tag=f"rt:mx{j}")
            self.S_(lambda e, lg=lg, nmx=nmx:
                    e.activation(self.oneh_sb[:], lg, ACT.Identity, bias=nmx[:, 0:1], scale=1.0),
                    r=[prr, "rmax"], w=["oneh"], tag=f"rt:dm{j}")
            self.V_(lambda e:
                    e.tensor_scalar(self.oneh_sb[:], self.oneh_sb[:], 0.0, None, ALU.is_ge),
                    r=["oneh"], w=["oneh"], tag=f"rt:oh{j}")
            self.S_(lambda e, lg=lg: e.activation(self.sig_sb[:], lg, ACT.Sigmoid),
                    r=[prr], w=["sig"], tag=f"rt:sg{j}")
            self.V_(lambda e: e.tensor_mul(self.spad[:, 64:80], self.oneh_sb[:], self.sig_sb[:]),
                    r=["oneh", "sig"], w=["spad"], tag=f"rt:s{j}")
            ps5 = self.ps[5]
            self.T_(lambda e, ps5=ps5: e.transpose(ps5[0:80, 0:128], self.spad[:, 0:80], self.identf[:]),
                    r=["spad", "identf"], w=["ps5"], tag=f"rt:t{j}")
            self.S_(lambda e, j=j, ps5=ps5:
                    e.activation(self.sTb[64:80, j * 128:(j + 1) * 128], ps5[64:80, 0:128], ACT.Copy),
                    r=["ps5"], w=["sTb"], tag=f"rt:e{j}")

        if getattr(self, "debug", False):
            self.dma("Gd", self.dbg_qt[:, 0:T], self.sTb[:], r=["sTb"], w=["dbg"], tag="dbg:sTb")
        self.V_(lambda e: e.memset(self.ACCB[:], 0.0), w=["ACCB"], tag="moe:z")
        self.moe_shared()
        if getattr(self, "debug", False):
            self.dma("Gd", self.dbg_xt, self.ACCB[:], r=["ACCB"], w=["dbg"], tag="dbg:accsh")
        self.moe_experts()
        if getattr(self, "debug", False):
            self.dma("Gd", self.dbg_ot, self.ACCB[:], r=["ACCB"], w=["dbg"], tag="dbg:accex")
        for j in range(NT):
            slot = j % 2
            xt_ap = self.xres_t[:, slot * C:(slot + 1) * C]
            self.dma("SYd", xt_ap, x_src[j * 128:(j + 1) * 128, :],
                     w=[f"xres_t{slot}"], tag=f"mf:lx{j}")
            xo = self.xn_f[:, slot * C:(slot + 1) * C]
            self.V_(lambda e, xo=xo, xt_ap=xt_ap, j=j:
                    e.tensor_add(xo, xt_ap, self.ACCB[:, j * 1024:(j + 1) * 1024]),
                    r=[f"xres_t{slot}", "ACCB"], w=[f"xn_f{slot}"], tag=f"mf:a{j}")
            self.dma("SYd", x_dst[j * 128:(j + 1) * 128, :], xo,
                     r=[f"xn_f{slot}"], w=[f"xdst{j}"], tag=f"mf:sx{j}")

    def moe_shared(self):
        k = self.k
        for E in range(8):
            for st in range(4):
                sfi = E * 4 + st
                wslot = sfi % 4
                gw = self.MWgu[:, wslot * 2048:wslot * 2048 + 1024]
                uw = self.MWgu[:, wslot * 2048 + 1024:wslot * 2048 + 2048]
                self.dma("Gd", gw, k["K_shgT"][:, sfi * 128:(sfi + 1) * 128]
                         .rearrange("(c p) n -> p c n", p=128),
                         w=[f"MWg{wslot}"], tag=f"sh:lg{sfi}")
                self.dma("Gd", uw, k["K_shuT"][:, sfi * 128:(sfi + 1) * 128]
                         .rearrange("(c p) n -> p c n", p=128),
                         w=[f"MWu{wslot}"], tag=f"sh:lu{sfi}")
                for qc in range(4):
                    psg, prg = self.ps[qc % 2], f"ps{qc % 2}"
                    psu, pru = self.ps[2 + qc % 2], f"ps{2 + qc % 2}"
                    for c in range(NC):
                        self.T_(lambda e, psg=psg, gw=gw, c=c, qc=qc:
                                e.matmul(psg[:], gw[:, c * 128:(c + 1) * 128],
                                         self.XT[:, c * T + qc * 512:c * T + (qc + 1) * 512],
                                         start=(c == 0), stop=(c == NC - 1)),
                                r=[f"MWg{wslot}", "XT"], w=[prg], tag=f"sh:g{sfi}.{qc}.{c}")
                    for c in range(NC):
                        self.T_(lambda e, psu=psu, uw=uw, c=c, qc=qc:
                                e.matmul(psu[:], uw[:, c * 128:(c + 1) * 128],
                                         self.XT[:, c * T + qc * 512:c * T + (qc + 1) * 512],
                                         start=(c == 0), stop=(c == NC - 1)),
                                r=[f"MWu{wslot}", "XT"], w=[pru], tag=f"sh:u{sfi}.{qc}.{c}")
                    self.S_(lambda e, psg=psg: e.activation(self.MOEsg[:], psg[:], ACT.Silu),
                            r=[prg], w=["MOEsg"], tag=f"sh:s{sfi}.{qc}")
                    dst = self.HSHE[:, (st * 4 + qc) * 512:(st * 4 + qc + 1) * 512]
                    self.V_(lambda e, dst=dst, psu=psu:
                            e.tensor_mul(dst, psu[:], self.MOEsg[:]),
                            r=[pru, "MOEsg"], w=["HSHE"], tag=f"sh:h{sfi}.{qc}")
            for c2 in range(4):
                for n in range(2):
                    wslot = (c2 * 2 + n) % 4
                    piece = self.MWd[:, wslot * 2048 + (n % 2) * 0:wslot * 2048 + 512] \
                        if False else self.MWd[:, ((c2 * 2 + n) % 8) * 1024:((c2 * 2 + n) % 8) * 1024 + 512]
                    pk = f"MWd{(c2 * 2 + n) % 8}"
                    self.dma("Gd", piece,
                             self.k["K_shdT"][(E * 4 + c2) * 128:(E * 4 + c2 + 1) * 128,
                                              n * 512:(n + 1) * 512],
                             w=[pk], tag=f"sh:ld{E}.{c2}.{n}")
            for j in range(NT):
                qc = j // 4
                jo = (j % 4) * 128
                for n in range(2):
                    psd, prd = self.ps[4 + n], f"ps{4 + n}"
                    for c2 in range(4):
                        piece = self.MWd[:, ((c2 * 2 + n) % 8) * 1024:((c2 * 2 + n) % 8) * 1024 + 512]
                        pk = f"MWd{(c2 * 2 + n) % 8}"
                        lhs = self.HSHE[:, (c2 * 4 + qc) * 512 + jo:(c2 * 4 + qc) * 512 + jo + 128]
                        self.T_(lambda e, psd=psd, lhs=lhs, piece=piece, c2=c2:
                                e.matmul(psd[:], lhs, piece, start=(c2 == 0), stop=(c2 == 3)),
                                r=["HSHE", pk], w=[prd], tag=f"sh:d{E}.{j}.{n}.{c2}")
                    a = self.ACCB[:, j * 1024 + n * 512:j * 1024 + (n + 1) * 512]
                    self.V_(lambda e, a=a, psd=psd: e.tensor_add(a, a, psd[:]),
                            r=[prd, "ACCB"], w=["ACCB"], tag=f"sh:a{E}.{j}.{n}")

    def moe_experts(self):
        k = self.k
        for n in range(NEXP):
            self.dma("Gd", self.MWd[:], k["K_down"][n].rearrange("(f p) h -> p f h", p=128),
                     w=["MWdN"] + [f"MWd{i}" for i in range(8)], tag=f"ex:ld{n}")
            for half in range(2):
                for qh in range(2):
                    qc = half * 2 + qh
                    ps5 = self.ps[5]
                    self.T_(lambda e, ps5=ps5, n=n, qc=qc:
                            e.matmul(ps5[:], self.nsel[64:80, n * 128:(n + 1) * 128],
                                     self.sTb[64:80, qc * 512:(qc + 1) * 512],
                                     start=True, stop=True),
                            r=["nsel", "sTb"], w=["ps5"], tag=f"ex:r{n}.{qc}")
                    for c in range(NC):
                        dst = self.X2Tn[:, c * 1024 + qh * 512:c * 1024 + (qh + 1) * 512]
                        self.V_(lambda e, dst=dst, c=c, qc=qc, ps5=ps5:
                                e.tensor_tensor(dst, self.XT[:, c * T + qc * 512:c * T + (qc + 1) * 512],
                                                ps5[:], ALU.mult),
                                r=["XT", "ps5"], w=["X2Tn"], tag=f"ex:p{n}.{qc}.{c}")
                for gt in range(8):
                    wslot = gt % 4
                    gw = self.MWgu[:, wslot * 2048:wslot * 2048 + 1024]
                    uw = self.MWgu[:, wslot * 2048 + 1024:wslot * 2048 + 2048]
                    self.dma("Gd", gw, k["K_gup"][n][:, gt * 128:(gt + 1) * 128]
                             .rearrange("(c p) m -> p c m", p=128),
                             w=[f"MWg{wslot}"], tag=f"ex:lg{n}.{half}.{gt}")
                    self.dma("Gd", uw, k["K_gup"][n][:, 1024 + gt * 128:1024 + (gt + 1) * 128]
                             .rearrange("(c p) m -> p c m", p=128),
                             w=[f"MWu{wslot}"], tag=f"ex:lu{n}.{half}.{gt}")
                    for qh in range(2):
                        psg, prg = self.ps[qh], f"ps{qh}"
                        psu, pru = self.ps[2 + qh], f"ps{2 + qh}"
                        for c in range(NC):
                            self.T_(lambda e, psg=psg, gw=gw, c=c, qh=qh:
                                    e.matmul(psg[:], gw[:, c * 128:(c + 1) * 128],
                                             self.X2Tn[:, c * 1024 + qh * 512:c * 1024 + (qh + 1) * 512],
                                             start=(c == 0), stop=(c == NC - 1)),
                                    r=[f"MWg{wslot}", "X2Tn"], w=[prg], tag=f"ex:g{n}.{half}.{gt}.{qh}.{c}")
                        for c in range(NC):
                            self.T_(lambda e, psu=psu, uw=uw, c=c, qh=qh:
                                    e.matmul(psu[:], uw[:, c * 128:(c + 1) * 128],
                                             self.X2Tn[:, c * 1024 + qh * 512:c * 1024 + (qh + 1) * 512],
                                             start=(c == 0), stop=(c == NC - 1)),
                                    r=[f"MWu{wslot}", "X2Tn"], w=[pru], tag=f"ex:u{n}.{half}.{gt}.{qh}.{c}")
                        self.S_(lambda e, psg=psg: e.activation(self.MOEsg[:], psg[:], ACT.Silu),
                                r=[prg], w=["MOEsg"], tag=f"ex:s{n}.{half}.{gt}.{qh}")
                        dst = self.MOEh[:, (gt * 2 + qh) * 512:(gt * 2 + qh + 1) * 512]
                        self.V_(lambda e, dst=dst, psu=psu:
                                e.tensor_mul(dst, psu[:], self.MOEsg[:]),
                                r=[pru, "MOEsg"], w=["MOEh"], tag=f"ex:h{n}.{half}.{gt}.{qh}")
                for jj in range(8):
                    j = half * 8 + jj
                    qh = jj // 4
                    jo = (jj % 4) * 128
                    for nn in range(2):
                        psd, prd = self.ps[4 + nn], f"ps{4 + nn}"
                        for f in range(8):
                            lhs = self.MOEh[:, (f * 2 + qh) * 512 + jo:(f * 2 + qh) * 512 + jo + 128]
                            rhs = self.MWd[:, f * 1024 + nn * 512:f * 1024 + (nn + 1) * 512]
                            self.T_(lambda e, psd=psd, lhs=lhs, rhs=rhs, f=f:
                                    e.matmul(psd[:], lhs, rhs, start=(f == 0), stop=(f == 7)),
                                    r=["MOEh", "MWdN"], w=[prd], tag=f"ex:d{n}.{j}.{nn}.{f}")
                        a = self.ACCB[:, j * 1024 + nn * 512:j * 1024 + (nn + 1) * 512]
                        self.V_(lambda e, a=a, psd=psd: e.tensor_add(a, a, psd[:]),
                                r=[prd, "ACCB"], w=["ACCB"], tag=f"ex:a{n}.{j}.{nn}")

    def l1_attn(self, x_src, x_dst):
        k = self.k
        self.v1_ones()

        def tile_cb(j, slot):
            v1r = self.V1[:].rearrange("p (jj h w) -> p jj h w", h=NH, w=65)
            xnb = self.xn_b[:, slot * C:(slot + 1) * C].rearrange("p (h w) -> p h w", h=16, w=64)
            self.S_(lambda e, j=j, xnb=xnb: e.activation(v1r[:, j, :, 0:64], xnb, ACT.Copy),
                    r=[f"xn_b{slot}"], w=["V1"], tag=f"l1:v{j}")

        self.ln_and_transpose(x_src, tile_cb=tile_cb, tag="ln2")
        for g in range(8):
            self.qk_group(g, k["K_W1qk"], rope=False, tag="a1")
            if g > 0:
                self.attn_group(g - 1, strictmask=False, tag="a1")
        self.attn_group(7, strictmask=False, tag="a1")
        self.attn_norm_finish("a1")
        if getattr(self, "debug", False):
            self.dma("Gd", self.dbg_qt[:, 0:T], self.OT[:, 0:T], r=["OT"], w=["dbg"], tag="dbg:y01")
            self.dma("Gd", self.dbg_kt[:, 0:T], self.OT[:, T:2 * T], r=["OT"], w=["dbg"], tag="dbg:y23")
        self.em.fence(lambda e: e.memset(self.fconst[:, 2:4], 0.0))
        ebr = self.Ebuf[:].rearrange("p (j hw) -> p j hw", j=NT, hw=NH * 64)
        v1r = self.V1[:].rearrange("p (j h w) -> p j h w", j=NT, h=NH, w=65)
        for j in range(NT):
            for c in range(NC):
                ii = (j * NC + c) % 2
                ps, pr = (self.psT1, self.psT2)[ii], f"psT{ii + 1}"
                self.T_(lambda e, c=c, j=j, ps=ps:
                        e.transpose(ps[:, 0:128], self.OT[:, c * T + j * 128:c * T + (j + 1) * 128],
                                    self.ident[:]),
                        r=["OT", "ident"], w=[pr], tag=f"l1:yt{j}.{c}")
                self.S_(lambda e, c=c, ps=ps:
                        e.activation(self.ytmp[:, c * 128:(c + 1) * 128], ps[:, 0:128], ACT.Copy),
                        r=[pr], w=["ytmp"], tag=f"l1:ye{j}.{c}")
            src_hi = v1r[:, j, :, 0:64]
            self.dma("SYd", self.vsh[0:127, :],
                     bass.AP(src_hi.tensor, src_hi.offset + src_hi.ap[0][0],
                             [[src_hi.ap[0][0], 127]] + list(src_hi.ap[1:])),
                     r=["V1"], w=["vsh"], tag=f"l1:vs{j}")
            jn = min(j + 1, NT - 1)
            rown = v1r[:, jn, :, 0:64]
            self.dma("SYd", self.vsh[127:128, :],
                     bass.AP(rown.tensor, rown.offset + (127 if jn == j else 0) * rown.ap[0][0],
                             [[rown.ap[0][0], 1]] + list(rown.ap[1:])),
                     r=["V1"], w=["vsh"], tag=f"l1:vn{j}")
            self.V_(lambda e, j=j: e.tensor_sub(ebr[:, j], self.vsh[:], self.ytmp[:]),
                    r=["vsh", "ytmp"], w=["Ebuf"], tag=f"l1:e{j}")
        for g in range(8):
            self.qk_group(g, k["K_W1qk"], rope=False, tag="b1",
                          wbuf=self.W1buf, wpfx="W1qk")
            self.k2_group(g)
            self.w_y2_group(g)
        if getattr(self, "debug", False):
            self.dma("Gd", self.dbg_xt, self.Ebuf[:], r=["Ebuf"], w=["dbg"], tag="dbg:ebuf")
            self.dma("Gd", self.dbg_ot, self.OT[:], r=["OT"], w=["dbg"], tag="dbg:ot1")
        self.em.fence(lambda e: e.memset(self.fconst[:, 2:4], 0.0))
        self.proj_residual("K_W1proj", None, x_src, x_dst, tag="p1")

    def k2_group(self, g):
        k = self.k
        slot = g % 2
        wslot = g % 2
        wsb = self.W2buf[:, wslot * 1024:(wslot + 1) * 1024]
        self.dma("Gd", wsb, k["K_W1k2"][:, g * 128:(g + 1) * 128]
                 .rearrange("(c p) n -> p c n", p=128),
                 w=[f"W1k2{wslot}"], tag=f"k2:lw{g}")
        for qc in range(4):
            ps, pr = self.ps[4 + qc % 2], f"ps{4 + qc % 2}"
            for c in range(NC):
                self.T_(lambda e, ps=ps, wsb=wsb, c=c, qc=qc:
                        e.matmul(ps[:], wsb[:, c * 128:(c + 1) * 128],
                                 self.XT[:, c * T + qc * 512:c * T + (qc + 1) * 512],
                                 start=(c == 0), stop=(c == NC - 1)),
                        r=[f"W1k2{wslot}", "XT"], w=[pr], tag=f"k2:m{g}.{qc}.{c}")
            dst = self.KA[:, qc * 512:(qc + 1) * 512]
            self.S_(lambda e, dst=dst, ps=ps:
                    e.activation(dst, ps[:], ACT.Sigmoid, scale=0.02 * 0.125),
                    r=[pr], w=["KA"], tag=f"k2:e{g}.{qc}")
        qt = self.QT[:, slot * T:(slot + 1) * T]
        self.V_(lambda e, qt=qt: e.tensor_scalar_mul(self.QA[:], qt, 0.125),
                r=[f"QT{slot}"], w=["QA"], tag=f"qa:{g}a")
        self.V_(lambda e: e.scalar_tensor_tensor(self.QA[:], self.QA[:], 0.02, self.QA[:],
                                                 ALU.mult, ALU.min),
                r=["QA"], w=["QA"], tag=f"qa:{g}b")

    def w_y2_group(self, g):
        ebr = self.Ebuf[:].rearrange("p (j hw) -> p j hw", j=NT, hw=NH * 64)
        for hh in range(2):
            h = 2 * g + hh
            hb = hh * 64
            cch = h // 2
            for qc in range(4):
                pso, po = self.ps[2 + (h * 4 + qc) % 2], f"ps{2 + (h * 4 + qc) % 2}"
                nkt = 4 * (qc + 1)
                for kti in range(nkt):
                    pss, pr = self.ps[kti % 2], f"ps{kti % 2}"
                    self.T_(lambda e, pss=pss, hb=hb, kti=kti, qc=qc:
                            e.matmul(pss[:], self.KA[hb:hb + 64, kti * 128:(kti + 1) * 128],
                                     self.QA[hb:hb + 64, qc * 512:(qc + 1) * 512],
                                     start=True, stop=True),
                            r=["QA", "KA"], w=[pr], tag=f"w1:s{h}.{qc}.{kti}")
                    pts = kti % 4
                    pt = self.PT[:, pts * 512:(pts + 1) * 512]
                    self.S_(lambda e, pt=pt, pss=pss: e.activation(pt, pss[:], ACT.Copy),
                            r=[pr], w=[f"PT{pts}"], tag=f"w1:c{h}.{qc}.{kti}")
                    m = kti - 4 * qc
                    if m >= 0:
                        mo = (4 + m) * 512
                        self.V_(lambda e, pt=pt, mo=mo:
                                e.tensor_mul(pt, pt, self.masks[:, mo:mo + 512]),
                                r=[f"PT{pts}", "masks"], w=[f"PT{pts}"], tag=f"w1:k{h}.{qc}.{kti}")
                    ev = ebr[:, kti, h * 64:(h + 1) * 64]
                    self.T_(lambda e, pso=pso, ev=ev, pt=pt, kti=kti, nkt=nkt:
                            e.matmul(pso[0:64, :], ev, pt, start=(kti == 0), stop=(kti == nkt - 1)),
                            r=[f"PT{pts}", "Ebuf"], w=[po], tag=f"w1:p{h}.{qc}.{kti}")
                dst = self.OT[hb:hb + 64, cch * T + qc * 512:cch * T + (qc + 1) * 512]
                self.V_(lambda e, dst=dst, pso=pso:
                        e.tensor_add(dst, dst, pso[0:64, :]),
                        r=[po, "OT"], w=["OT"], tag=f"w1:o{h}.{qc}")

    def l1ffn_weight_quant(self):
        k = self.k
        srcs = [k["K_wgT"], k["K_wuT"], k["K_wdT"]]
        strip = self.sums[:, 0:32]
        acc = self.stats[:, 32:34]
        pad = self.stats[:, 62:64]

        def tile_src(wi, t_):
            if wi < 2:
                return srcs[wi][:, t_ * 128:(t_ + 1) * 128].rearrange("(c p) n -> p c n", p=128)
            return srcs[2][t_ * 128:(t_ + 1) * 128, :]

        def tile_dst(wi, t_):
            if wi < 2:
                return self.nw_dram[wi][:, t_ * 128:(t_ + 1) * 128].rearrange("(c p) n -> p c n", p=128)
            return self.nw_dram[2][t_ * 128:(t_ + 1) * 128, :]

        bufs = [self.xn_f[:, 0:1024], self.xn_f[:, 1024:2048]]
        for wi in range(3):
            for t_ in range(32):
                slot = t_ % 2
                buf = bufs[slot]
                bk = f"wqin{slot}"
                self.dma("Gd", buf, tile_src(wi, t_), w=[bk], tag=f"wq:l{wi}.{t_}")
                self.V_(lambda e, buf=buf, t_=t_:
                        e.reduce_sum(strip[:, t_:t_ + 1], buf, AX, apply_absolute_value=True),
                        r=[bk], w=["wqstrip"], tag=f"wq:r{wi}.{t_}")
            self.V_(lambda e: e.memset(pad, 0.0), tag=f"wq:p{wi}a")
            self.V_(lambda e: e.memset(pad, 0.0), tag=f"wq:p{wi}b")
            self.V_(lambda e: e.reduce_sum(acc[:, 0:1], strip, AX),
                    r=["wqstrip"], w=["wqacc"], tag=f"wq:a{wi}")
            ps5 = self.ps[5]
            self.T_(lambda e, ps5=ps5: e.matmul(ps5[0:2, 0:2], acc, self.ones128[:, 0:2],
                                                start=True, stop=True),
                    r=["wqacc", "ones128"], w=["ps5"], tag=f"wq:t{wi}")
            mn = self.stats[:, 48:50]
            self.S_(lambda e, ps5=ps5: e.activation(mn[0:1, :], ps5[0:1, 0:2], ACT.Copy,
                                                    scale=1.0 / (INNER * 1024.0)),
                    r=["ps5"], w=["wqmn"], tag=f"wq:m{wi}")
            self.V_(lambda e: e.tensor_scalar_max(mn[0:1, :], mn[0:1, :], 1e-5),
                    r=["wqmn"], w=["wqmn"], tag=f"wq:c{wi}")
            self.T_(lambda e, ps5=ps5: e.matmul(ps5[:, 0:2], self.ones128[0:1, :], mn[0:1, :],
                                                start=True, stop=True),
                    r=["wqmn", "ones128"], w=["ps5"], tag=f"wq:rp{wi}")
            self.S_(lambda e, wi=wi, ps5=ps5:
                    e.activation(self.theta[:, 2 * wi:2 * wi + 2], ps5[:, 0:2], ACT.Copy),
                    r=["ps5"], w=["theta"], tag=f"wq:th{wi}")
            self.V_(lambda e, wi=wi:
                    e.tensor_scalar_mul(self.theta[:, 8 + 2 * wi:10 + 2 * wi],
                                        self.theta[:, 2 * wi:2 * wi + 2], -0.5),
                    r=["theta"], w=["theta"], tag=f"wq:th2{wi}")
            tma = self.brep[:]
            tmb = self.brep2[:]
            tm = self.ropeQ[:, 0:1024]
            tu = self.ropeQ[:, 1024:2048]
            wqo = self.ropeA[:, 0:1024]
            nth = self.theta[:, 8 + 2 * wi:9 + 2 * wi]
            for t_ in range(32):
                slot = t_ % 2
                buf = bufs[slot]
                bk = f"wqin{slot}"
                self.dma("Gd", buf, tile_src(wi, t_), w=[bk], tag=f"wq:l2{wi}.{t_}")
                self.V_(lambda e, buf=buf:
                        e.scalar_tensor_tensor(tma, buf, -1.0, buf, ALU.mult, ALU.max),
                        r=[bk], w=["wqtma"], tag=f"wq:ab{wi}.{t_}")
                self.S_(lambda e, nth=nth:
                        e.activation(tmb, tma, ACT.Identity, bias=nth, scale=1.0),
                        r=["wqtma", "theta"], w=["wqtmb"], tag=f"wq:sb{wi}.{t_}")
                self.V_(lambda e:
                        e.tensor_scalar(tm, tmb, 0.0, None, ALU.is_ge),
                        r=["wqtmb"], w=["wqtm"], tag=f"wq:ge{wi}.{t_}")
                self.V_(lambda e, buf=buf:
                        e.tensor_scalar(tu, buf, 1e30, -1.0, ALU.mult, ALU.max),
                        r=[bk], w=["wqtu"], tag=f"wq:u{wi}.{t_}")
                self.V_(lambda e:
                        e.scalar_tensor_tensor(wqo, tu, 1.0, tm, ALU.min, ALU.mult),
                        r=["wqtm", "wqtu"], w=["wq_o"], tag=f"wq:n{wi}.{t_}")
                self.dma("SYd", tile_dst(wi, t_), wqo,
                         r=["wq_o"], w=[f"nwdram{wi}"], tag=f"wq:s{wi}.{t_}")

    def l1_ffn(self, x_src, x_dst):
        k = self.k
        for j in range(NT):
            slot = j % 2
            xt_ap = self.xres_t[:, slot * C:(slot + 1) * C]
            self.dma("SYd", xt_ap, x_src[j * 128:(j + 1) * 128, :],
                     w=[f"xres_t{slot}"], tag=f"f1:lx{j}")
            self.ln_tile(xt_ap, slot, True, f"f1ln{j}")
            xnf = self.xn_f[:, slot * C:(slot + 1) * C]
            am = self.csc[:, 0:2]
            pad = self.stats[:, 62:64]
            self.V_(lambda e, xnf=xnf, am=am: e.reduce_max(am[:, 0:1], xnf, AX, apply_absolute_value=True),
                    r=[f"xn_f{slot}"], w=["am"], tag=f"f1:am{j}")
            self.V_(lambda e: e.memset(pad, 0.0), tag=f"f1:pa{j}")
            self.V_(lambda e: e.memset(pad, 0.0), tag=f"f1:pb{j}")
            self.V_(lambda e, am=am: e.tensor_scalar_max(am, am, 1e-5), r=["am"], w=["am"], tag=f"f1:cl{j}")
            cx = self.csc[:, 2:4]
            self.V_(lambda e, am=am, cx=cx: e.tensor_scalar_mul(cx, am, 1.0 / 127.0),
                    r=["am"], w=["cx"], tag=f"f1:cx{j}")
            sx = self.csc[:, 4:6]
            self.V_(lambda e, cx=cx, sx=sx: e.reciprocal(sx, cx),
                    r=["cx"], w=["sx"], tag=f"f1:sx{j}")
            self.quant_chain(xnf, f"xn_f{slot}", sx[:, 0:1], "sx", self.XQ[:], "XQ", 1024, f"f1:q{j}")
            for c in range(NC):
                ii = (j * NC + c) % 2
                ps, pr = (self.psT1, self.psT2)[ii], f"psT{ii + 1}"
                self.T_(lambda e, c=c, ps=ps:
                        e.transpose(ps[:, 0:128], self.XQ[:, c * 128:(c + 1) * 128], self.ident[:]),
                        r=["XQ", "ident"], w=[pr], tag=f"f1:t{j}.{c}")
                dst = self.XT[:, c * T + j * 128:c * T + (j + 1) * 128]
                self.S_(lambda e, dst=dst, ps=ps: e.activation(dst, ps[:, 0:128], ACT.Copy),
                        r=[pr], w=["XT"], tag=f"f1:te{j}.{c}")
            cg = self.csc[:, 12 + slot * 2:14 + slot * 2]
            cu = self.csc[:, 16 + slot * 2:18 + slot * 2]
            self.V_(lambda e, cg=cg, cx=cx: e.tensor_mul(cg, cx, self.theta[:, 0:2]),
                    r=["cx", "theta"], w=[f"cg{slot}"], tag=f"f1:cg{j}")
            self.V_(lambda e, cu=cu, cx=cx: e.tensor_mul(cu, cx, self.theta[:, 2:4]),
                    r=["cx", "theta"], w=[f"cu{slot}"], tag=f"f1:cu{j}")
            h2 = self.H2[:, (j % 2) * INNER:(j % 2 + 1) * INNER]
            for gn in range(8):
                psg, prg = self.ps[0 + gn % 2], f"ps{gn % 2}"
                psu, pru = self.ps[2 + gn % 2], f"ps{2 + gn % 2}"
                for c in range(NC):
                    wsl = c % 2
                    gpiece = self.NWgu[:, wsl * 512:(wsl + 1) * 512]
                    upiece = self.NWgu[:, (2 + wsl) * 512:(3 + wsl) * 512]
                    self.dma("Gd", gpiece,
                             self.nw_dram[0][c * 128:(c + 1) * 128, gn * 512:(gn + 1) * 512],
                             r=["nwdram0"], w=[f"NWg{wsl}"], tag=f"f1:lg{j}.{gn}.{c}")
                    self.dma("Gd", upiece,
                             self.nw_dram[1][c * 128:(c + 1) * 128, gn * 512:(gn + 1) * 512],
                             r=["nwdram1"], w=[f"NWu{wsl}"], tag=f"f1:lu{j}.{gn}.{c}")
                    lhs = self.XT[:, c * T + j * 128:c * T + (j + 1) * 128]
                    self.T_(lambda e, psg=psg, lhs=lhs, gpiece=gpiece, c=c:
                            e.matmul(psg[:], lhs, gpiece, start=(c == 0), stop=(c == NC - 1)),
                            r=["XT", f"NWg{wsl}"], w=[prg], tag=f"f1:g{j}.{gn}.{c}")
                    self.T_(lambda e, psu=psu, lhs=lhs, upiece=upiece, c=c:
                            e.matmul(psu[:], lhs, upiece, start=(c == 0), stop=(c == NC - 1)),
                            r=["XT", f"NWu{wsl}"], w=[pru], tag=f"f1:u{j}.{gn}.{c}")
                sgl = self.SILG[:, (gn % 2) * 512:(gn % 2 + 1) * 512]
                self.S_(lambda e, psg=psg, sgl=sgl, cg=cg: e.activation(sgl, psg[:], ACT.Silu, scale=cg[:, 0:1]),
                        r=[prg, f"cg{slot}"], w=[f"SILG{gn % 2}"], tag=f"f1:s{j}.{gn}")
                dh = h2[:, gn * 512:(gn + 1) * 512]
                self.V_(lambda e, dh=dh, psu=psu, sgl=sgl:
                        e.tensor_mul(dh, psu[:], sgl),
                        r=[pru, f"SILG{gn % 2}"], w=[f"H2_{j % 2}"], tag=f"f1:h{j}.{gn}")
            ah = self.csc[:, 6:8]
            self.V_(lambda e, h2=h2, ah=ah: e.reduce_max(ah[:, 0:1], h2, AX, apply_absolute_value=True),
                    r=[f"H2_{j % 2}"], w=["ah"], tag=f"f1:ah{j}")
            self.V_(lambda e: e.memset(pad, 0.0), tag=f"f1:pc{j}")
            self.V_(lambda e: e.memset(pad, 0.0), tag=f"f1:pd{j}")
            self.V_(lambda e, ah=ah: e.tensor_scalar_max(ah, ah, 1e-5), r=["ah"], w=["ah"], tag=f"f1:ac{j}")
            chh = self.csc[:, 8:10]
            self.V_(lambda e, ah=ah, chh=chh: e.tensor_scalar_mul(chh, ah, 1.0 / 127.0),
                    r=["ah"], w=["ch"], tag=f"f1:ch{j}")
            shh = self.csc[:, 10:12]
            self.V_(lambda e, chh=chh, shh=shh: e.reciprocal(shh, chh),
                    r=["ch"], w=["sh"], tag=f"f1:sh{j}")
            for pc in range(4):
                self.quant_chain(h2[:, pc * 1024:(pc + 1) * 1024], f"H2_{j % 2}", shh[:, 0:1], "sh",
                                 self.NHb[:, pc * 1024:(pc + 1) * 1024], "NHb", 1024, f"f1:qh{j}.{pc}")
            nslot = j % 2
            for c2 in range(32):
                ii = c2 % 2
                ps, pr = (self.psT1, self.psT2)[ii], f"psT{ii + 1}"
                self.T_(lambda e, c2=c2, ps=ps:
                        e.transpose(ps[:, 0:128], self.NHb[:, c2 * 128:(c2 + 1) * 128], self.ident[:]),
                        r=["NHb", "ident"], w=[pr], tag=f"f1:nt{j}.{c2}")
                dst = self.NHT[:, nslot * 4096 + c2 * 128:nslot * 4096 + (c2 + 1) * 128]
                self.S_(lambda e, dst=dst, ps=ps: e.activation(dst, ps[:, 0:128], ACT.Copy),
                        r=[pr], w=[f"NHT{nslot}"], tag=f"f1:ne{j}.{c2}")
            cf = self.csc[:, 20 + nslot * 2:22 + nslot * 2]
            self.V_(lambda e, cf=cf, chh=chh, cu=cu: e.tensor_mul(cf, chh, cu),
                    r=["ch", f"cu{slot}"], w=[f"cf{nslot}"], tag=f"f1:cf{j}")
            self.V_(lambda e, cf=cf: e.tensor_mul(cf, cf, self.theta[:, 4:6]),
                    r=[f"cf{nslot}", "theta"], w=[f"cf{nslot}"], tag=f"f1:cf2{j}")
            for nn in range(2):
                psd, prd = self.ps[4 + nn], f"ps{4 + nn}"
                for c2 in range(32):
                    wsl = c2 % 2
                    piece = self.NWd[:, (nn * 2 + wsl) * 512:(nn * 2 + wsl + 1) * 512]
                    pk = f"NWd{nn}_{wsl}"
                    self.dma("Gd", piece,
                             self.nw_dram[2][c2 * 128:(c2 + 1) * 128, nn * 512:(nn + 1) * 512],
                             r=["nwdram2"], w=[pk], tag=f"f1:ld{j}.{nn}.{c2}")
                    lhs = self.NHT[:, nslot * 4096 + c2 * 128:nslot * 4096 + (c2 + 1) * 128]
                    self.T_(lambda e, psd=psd, lhs=lhs, piece=piece, c2=c2:
                            e.matmul(psd[:], lhs, piece, start=(c2 == 0), stop=(c2 == 31)),
                            r=[f"NHT{nslot}", pk], w=[prd], tag=f"f1:d{j}.{nn}.{c2}")
                xo = self.xn_f[:, slot * C:(slot + 1) * C]
                seg = slice(nn * 512, (nn + 1) * 512)
                self.S_(lambda e, psd=psd, cf=cf:
                        e.activation(self.MOEsg[:], psd[:], ACT.Identity, scale=cf[:, 0:1]),
                        r=[prd, f"cf{nslot}"], w=["MOEsg"], tag=f"f1:sc{j}.{nn}")
                self.V_(lambda e, seg=seg, xo=xo, xt_ap=xt_ap:
                        e.tensor_add(xo[:, seg], self.MOEsg[:], xt_ap[:, seg]),
                        r=["MOEsg", f"xres_t{slot}"], w=[f"xn_f{slot}"],
                        tag=f"f1:o{j}.{nn}")
            self.dma("SYd", x_dst[j * 128:(j + 1) * 128, :],
                     self.xn_f[:, slot * C:(slot + 1) * C],
                     r=[f"xn_f{slot}"], w=[f"xdst{j}"], tag=f"f1:sx{j}")

    def quant_chain(self, y_src, ykey, scale_ap, skey, out_ap, okey, width, tag):
        q1 = self.Q1[:, 0:width]
        q2 = self.Q2[:, 0:width]
        q3 = self.Q3[:, 0:width]
        q4 = self.Q4[:, 0:width]
        q5 = self.Q5[:, 0:width]
        self.S_(lambda e: e.activation(q3, y_src, ACT.Identity, scale=scale_ap),
                r=[ykey, skey], w=["q3"], tag=tag + ":y")
        self.V_(lambda e: e.tensor_copy(q1, q3), r=["q3"], w=["q1"], tag=tag + ":i")
        self.V_(lambda e: e.tensor_copy(q2, q1), r=["q1"], w=["q2"], tag=tag + ":f")
        self.V_(lambda e: e.scalar_tensor_tensor(q4, q3, -0.5, q2, ALU.add, ALU.is_ge),
                r=["q3", "q2"], w=["q4"], tag=tag + ":m1")
        self.V_(lambda e: e.scalar_tensor_tensor(q5, q3, 0.5, q2, ALU.add, ALU.is_le),
                r=["q3", "q2"], w=["q5"], tag=tag + ":m2")
        self.V_(lambda e: e.tensor_add(q4, q4, q2), r=["q4", "q2"], w=["q4"], tag=tag + ":t")
        self.V_(lambda e: e.scalar_tensor_tensor(out_ap, q5, -1.0, q4, ALU.mult, ALU.add),
                r=["q5", "q4"], w=[okey], tag=tag + ":u")

    def build(self):
        self.load_consts()
        st = self.stages
        dst0 = self.out if st == 1 else self.xres_dram
        self.l0_attn(dst0)
        if st >= 2:
            if st >= 4:
                self.em.fence(lambda e: e.memset(self.fconst[:, 2:4], 0.0))
                self.l1ffn_weight_quant()
            self.em.fence(lambda e: e.memset(self.fconst[:, 2:4], 0.0))
            dst1 = self.out if st == 2 else self.xres_dram
            self.moe(self.xres_dram, dst1)
        if st >= 3:
            self.em.fence(lambda e: e.memset(self.fconst[:, 2:4], 0.0))
            dst2 = self.out if st == 3 else self.xres_dram
            self.l1_attn(self.xres_dram, dst2)
        if st >= 4:
            self.em.fence(lambda e: e.memset(self.fconst[:, 2:4], 0.0))
            self.l1_ffn(self.xres_dram, self.out)
        self.em.finalize()

    def emit_blocks(self):
        nc = self.nc
        import contextlib
        with contextlib.ExitStack() as stk:
            sems = {}
            for ch in ("T", "V", "S", "G", "SY"):
                sems[ch] = stk.enter_context(nc.semaphore(f"sem_{ch}"))
            for ch in DMACH:
                for i in range(NSUB):
                    sems[(ch, i)] = stk.enter_context(nc.semaphore(f"sem_{ch}{i}"))
            blk = stk.enter_context(nc.Block())

            @blk.tensor
            def _(eng):
                self.em.replay("tensor", eng, sems)

            @blk.vector
            def _(eng):
                self.em.replay("vector", eng, sems)

            @blk.scalar
            def _(eng):
                self.em.replay("scalar", eng, sems)

            @blk.gpsimd
            def _(eng):
                self.em.replay("gpsimd", eng, sems)

            @blk.sync
            def _(eng):
                self.em.replay("sync", eng, sems)
                for key, n in self.em.sub_totals.items():
                    eng.wait_ge(sems[key], n * 16)


def build_nc(stages=4):
    nc = bass.Bass("TRN2", target_bir_lowering=False, detect_race_conditions=False)
    mk = MK(nc, stages)
    mk.build()
    mk.emit_blocks()
    return nc


_NC_CACHE = {}


def kernel(**inputs):
    import os
    stages = int(os.environ.get("MK_STAGES", "4"))
    if stages not in _NC_CACHE:
        _NC_CACHE[stages] = build_nc(stages)
    nc = _NC_CACHE[stages]
    consts = stage_consts(inputs)
    x = np.asarray(inputs["L0_attn_x"], dtype=np.float32)
    in_maps = []
    for core in range(8):
        m = dict(consts)
        m["x"] = np.ascontiguousarray(x[core])
        in_maps.append(m)
    res = run_bass_kernel_spmd(nc, in_maps, core_ids=list(range(8)))
    out = np.stack([np.asarray(r["out"]) for r in res.results], axis=0)
    return out.astype(np.float32)
